# version 02 (final)
# baseline (speedup 1.0000x reference)
import numpy as np
import concourse.bass as bass
import concourse.mybir as mybir
from concourse.bass_utils import run_bass_kernel_spmd

F32 = mybir.dt.float32
I32 = mybir.dt.int32
AF = mybir.ActivationFunctionType
ALU = mybir.AluOpType

D = 4096
S = 2048
SO = 1024
RSQ_HD = float(np.float32(1.0 / np.sqrt(128.0)))
EPS = 1e-5
ENGINES = ("sync", "scalar", "vector", "tensor", "gpsimd")


class B:

    def __init__(self, nc):
        self.nc = nc
        self.ops = {e: [] for e in ENGINES}
        self.cnt = {e: 0 for e in ENGINES}
        self.serial = False
        self.last = None

    def op(self, eng, fn, waits=(), inc=1):
        waits = [w for w in waits if w is not None]
        if self.serial and self.last is not None:
            waits.append(self.last)
        self.ops[eng].append((tuple(waits), fn, inc))
        self.cnt[eng] += inc
        mark = (eng, self.cnt[eng])
        if inc:
            self.last = mark
        return mark

    def dma(self, eng, out, in_, waits=()):
        return self.op(eng, lambda e, o=out, i=in_: e.dma_start(out=o, in_=i),
                       waits, 16)

    def wait_only(self, eng, waits):
        self.ops[eng].append((tuple(waits), None, 0))

    def emit(self, block, sems):
        for eng in ENGINES:
            if not self.ops[eng]:
                continue
            section = getattr(block, eng)
            ops = self.ops[eng]

            def body(e, eng=eng, ops=ops):
                seen = {x: 0 for x in ENGINES}
                for waits, fn, inc in ops:
                    for (weng, wval) in waits:
                        if weng == eng:
                            continue
                        if wval > seen[weng]:
                            e.wait_ge(sems[weng], wval)
                            seen[weng] = wval
                    if fn is None:
                        continue
                    last = fn(e)
                    if inc:
                        last.then_inc(sems[eng], inc)

            section(body)


def make_sems_and_emit(nc, b):
    with (
        nc.semaphore("hsy") as hsy,
        nc.semaphore("hsq") as hsq,
        nc.semaphore("hgp") as hgp,
        nc.semaphore("hv") as hv,
        nc.semaphore("hsc") as hsc,
        nc.semaphore("ht") as ht,
        nc.Block() as block,
    ):
        sems = {"sync": hsy, "scalar": hsc, "vector": hv, "tensor": ht,
                "gpsimd": hgp}
        b.emit(block, sems)
    return nc


def ln_tile(b, bufs, x_tile, xn_tile, dep_first_scalar, dep_norm, n):
    st = bufs["st"]
    scr = bufs.get("scr", xn_tile)
    b.op("scalar", lambda e: e.activation(scr, x_tile, AF.Copy,
                                          accum_out=st["musum"]),
         dep_first_scalar, 0)
    m_sq = b.op("scalar", lambda e: e.activation(scr, x_tile, AF.Square,
                                                 accum_out=st["sqsum"]))
    m_var = b.op("vector", lambda e: (
        e.tensor_scalar(st["mu"], st["musum"], 1.0 / n, None, ALU.mult),
        e.tensor_scalar(st["t1"], st["sqsum"], 1.0 / n, EPS, ALU.mult, ALU.add),
        e.tensor_tensor(st["t2"], st["mu"], st["mu"], ALU.mult),
        e.tensor_copy(st["musum"], st["t1"]),
        e.tensor_tensor(st["t1"], st["t1"], st["t2"], ALU.subtract))[-1],
        [m_sq])
    m_sqrt = b.op("scalar", lambda e: e.activation(st["t2"], st["t1"], AF.Sqrt),
                  [m_var])
    m_norm = b.op("vector", lambda e: (
        e.reciprocal(st["r"], st["t2"]),
        e.tensor_copy(st["musum"], st["mu"]),
        e.tensor_tensor(st["t2"], st["r"], st["r"], ALU.mult),
        e.tensor_copy(st["sqsum"], st["mu"]),
        e.tensor_tensor(st["t2"], st["t2"], st["t1"], ALU.mult),
        e.tensor_copy(st["sqsum"], st["mu"]),
        e.tensor_scalar(st["t2"], st["t2"], -0.5, 1.5, ALU.mult, ALU.add),
        e.tensor_copy(st["sqsum"], st["mu"]),
        e.tensor_tensor(st["r"], st["r"], st["t2"], ALU.mult),
        e.tensor_copy(st["sqsum"], st["mu"]),
        e.tensor_scalar(xn_tile, x_tile, st["mu"], st["r"],
                        ALU.subtract, ALU.mult))[-1],
        [m_sqrt] + list(dep_norm))
    return m_var, m_norm


class LnState:

    def __init__(self):
        self.m_var_prev = None
        self.m_xt_prev = [None]
        self.m_xnt_prev = None
        self.evacs = []
        self.dep_first_evac = []


def ln_transpose_block(b, bufs, ls, src_dram, row0, arena, psT, t_eye, n_tiles=8):
    xt, xnt = bufs["xt"], bufs["xnt"]
    last = None
    for tt in range(8)[:n_tiles]:
        m_ld = b.dma("sync", xt[0],
                     src_dram[row0 + tt * 128: row0 + (tt + 1) * 128, :],
                     [ls.m_xt_prev[0]])
        m_var, m_norm = ln_tile(b, bufs, xt[0], xnt,
                                [m_ld, ls.m_var_prev, ls.m_xnt_prev],
                                [], D)
        ls.m_var_prev = m_var
        ls.m_xt_prev[0] = m_norm
        for ic in range(32):
            w = [m_norm]
            if len(ls.evacs) >= 2:
                w.append(ls.evacs[-2])
            m_tr = b.op("tensor", lambda e, ic=ic: e.transpose(
                psT[ic % 2][:, 0:128], xnt[:, ic * 128:(ic + 1) * 128], t_eye), w)
            w = [m_tr] + ls.dep_first_evac
            ls.dep_first_evac = []
            last = b.op("vector", lambda e, ic=ic, tt=tt: e.tensor_copy(
                arena[:, ic * SO + tt * 128: ic * SO + (tt + 1) * 128],
                psT[ic % 2][:, 0:128]), w)
            ls.evacs.append(last)
        ls.m_xnt_prev = m_tr
    return last


class PsumState:

    def __init__(self):
        self.ev = []
        self.sto = []
        self.wrel = []


def project(b, bufs, ps, w_dram, arena, out_dram, n_tok, feature_major,
            dep_arena, dep_w=(), out_col0=0, out_row0=0):
    wb, ost, psA = bufs["wb"], bufs["ost"], bufs["psA"]
    wr = w_dram.rearrange("(c p) o -> p c o", p=128)
    m_wload = {}
    for ob in range(16):
        w = list(dep_w)
        if ps.wrel:
            w.append(ps.wrel[-1])
        dst = wb.rearrange("p (c o) -> p c o", c=32)
        b.dma("sync", dst[:, 0:16, :], wr[:, 0:16, ob * 256:(ob + 1) * 256], w)
        m_wload[ob] = b.dma("sync", dst[:, 16:32, :],
                            wr[:, 16:32, ob * 256:(ob + 1) * 256])
        n_chain = (2 * (n_tok // 512)) if feature_major else (n_tok // 128)
        m_mm = None
        for ch in range(n_chain):
            w = [m_wload[ob], dep_arena]
            if len(ps.ev) >= 2:
                w.append(ps.ev[-2])
            ci = len(ps.ev)

            def mms(e, ob=ob, ch=ch, ci=ci):
                if feature_major:
                    oc, qt = ch // 2, ch % 2
                    for ic in range(32):
                        last = e.matmul(
                            psA[ci % 2],
                            wb[:, ic * 256 + oc * 128:
                               ic * 256 + (oc + 1) * 128],
                            arena[:, ic * SO + qt * 512: ic * SO + (qt + 1) * 512],
                            start=(ic == 0), stop=(ic == 31))
                else:
                    for ic in range(32):
                        last = e.matmul(
                            psA[ci % 2][:, 0:256],
                            arena[:, ic * SO + ch * 128: ic * SO + (ch + 1) * 128],
                            wb[:, ic * 256:(ic + 1) * 256],
                            start=(ic == 0), stop=(ic == 31))
                return last

            m_mm = b.op("tensor", mms, w)
            width = 512 if feature_major else 256
            w = [m_mm]
            if len(ps.sto) >= 2:
                w.append(ps.sto[-2])
            m_ev = b.op("vector", lambda e, ci=ci, width=width: e.tensor_copy(
                ost[ci % 2][:, 0:width], psA[ci % 2][:, 0:width]), w)
            ps.ev.append(m_ev)
            if feature_major:
                oc, qt = ch // 2, ch % 2
                dst = out_dram[ob * 256 + oc * 128: ob * 256 + (oc + 1) * 128,
                               out_col0 + qt * 512: out_col0 + (qt + 1) * 512]
            else:
                dst = out_dram[out_row0 + ch * 128: out_row0 + (ch + 1) * 128,
                               ob * 256:(ob + 1) * 256]
            ps.sto.append(b.dma("scalar", dst, ost[ci % 2][:, 0:width], [m_ev]))
        ps.wrel.append(m_mm)
    return ps.sto[-1]


def build_launch_a(debug=False):
    nc = bass.Bass("TRN2", target_bir_lowering=False)
    xb = nc.dram_tensor("xb", [S, D], F32, kind="ExternalInput").ap()
    wqT = nc.dram_tensor("wqT", [D, D], F32, kind="ExternalInput").ap()
    wkT = nc.dram_tensor("wkT", [D, D], F32, kind="ExternalInput").ap()
    wvT = nc.dram_tensor("wvT", [D, D], F32, kind="ExternalInput").ap()
    woT = nc.dram_tensor("woT", [D, D], F32, kind="ExternalInput").ap()
    qTi = nc.dram_tensor("qTi", [D, D], I32, kind="ExternalInput").ap()
    scT = nc.dram_tensor("scT", [32, D], F32, kind="ExternalInput").ap()
    szT = nc.dram_tensor("szT", [32, D], F32, kind="ExternalInput").ap()
    fbias = nc.dram_tensor("fbias", [1, D], F32, kind="ExternalInput").ap()
    eye = nc.dram_tensor("eye", [128, 128], F32, kind="ExternalInput").ap()
    onesc = nc.dram_tensor("onesc", [128, 1], F32, kind="ExternalInput").ap()
    onesr = nc.dram_tensor("onesr", [1, 128], F32, kind="ExternalInput").ap()
    x1 = nc.dram_tensor("x1", [SO, D], F32, kind="ExternalOutput").ap()
    dbg = "ExternalOutput" if debug else "Internal"
    wt_d = nc.dram_tensor("wt_d", [D, D], F32, kind=dbg).ap()
    qt_d = nc.dram_tensor("qt_d", [D, SO], F32, kind=dbg).ap()
    kt_d = nc.dram_tensor("kt_d", [D, S], F32, kind=dbg).ap()
    v_d = nc.dram_tensor("v_d", [S, D], F32, kind=dbg).ap()
    f_d = nc.dram_tensor("f_d", [SO, D], F32, kind=dbg).ap()

    arena = nc.alloc_sbuf_tensor("arena", [128, 32 * SO], F32).ap()
    wbuf = nc.alloc_sbuf_tensor("wbuf", [128, 32 * 256], F32).ap()
    aux = nc.alloc_sbuf_tensor("aux", [128, 2 * D], F32).ap()
    bufs = {
        "wb": wbuf,
        "xt": [aux[:, 0:D]],
        "xnt": aux[:, D:2 * D],
        "ost": [nc.alloc_sbuf_tensor(f"ost{i}", [128, 512], F32).ap()
                for i in (0, 1)],
        "st": {k: nc.alloc_sbuf_tensor("st_" + k, [128, 1], F32).ap()
               for k in ("musum", "sqsum", "mu", "r", "t1", "t2")},
    }
    t_eye = nc.alloc_sbuf_tensor("t_eye", [128, 128], F32).ap()
    t_ones = nc.alloc_sbuf_tensor("t_ones", [128, 1], F32).ap()
    t_onesr = nc.alloc_sbuf_tensor("t_onesr", [1, 128], F32).ap()
    kh = aux[:, 0:2048]
    qh = aux[:, 2048:3072]
    vh = aux[:, 3072:5120]
    pt = [aux[:, 5120:5632], aux[:, 5632:6144]]
    rzb = aux[:, 6144:6656]
    xsl = aux[:, 6656:6912]
    fsl = aux[:, 6912:7168]
    fbb = aux[:, 7168:7424]
    rz = nc.alloc_sbuf_tensor("rz", [1, 512], F32).ap()

    psA = [nc.alloc_psum_tensor(f"psA{i}", [128, 512], F32).ap() for i in (0, 1)]
    psS = [nc.alloc_psum_tensor(f"psS{i}", [128, 512], F32).ap() for i in (0, 1)]
    psO = nc.alloc_psum_tensor("psO", [128, 512], F32).ap()
    psZ = nc.alloc_psum_tensor("psZ", [1, 512], F32).ap()
    psB = nc.alloc_psum_tensor("psB", [128, 512], F32).ap()
    bufs["psA"] = psA

    b = B(nc)
    m_eye = b.dma("sync", t_eye, eye)
    m_ones = b.dma("sync", t_ones, onesc)
    m_onesr = b.dma("sync", t_onesr, onesr)

    scb = wbuf[:, 0:D]
    szb = wbuf[:, D:2 * D]
    dq_v, dq_sq = [], []
    for ic in range(32):
        sl = ic % 2
        w = [dq_v[ic - 1]] if ic >= 1 else []
        m_q = b.dma("gpsimd", bufs["xt"][0], qTi[ic * 128:(ic + 1) * 128, :], w)
        b.dma("sync", scb, scT[ic:ic + 1, :].partition_broadcast(128), w)
        m_sz = b.dma("sync", szb, szT[ic:ic + 1, :].partition_broadcast(128))
        w2 = [m_q, m_sz]
        if ic >= 2:
            w2.append(dq_sq[ic - 2])
        stg = arena[:, sl * D:sl * D + D]
        m_v = b.op("vector", lambda e, stg=stg: (
            e.tensor_tensor(stg, bufs["xt"][0], scb, ALU.mult),
            e.tensor_tensor(stg, stg, szb, ALU.add))[-1], w2)
        dq_v.append(m_v)
        dq_sq.append(b.dma("scalar", wt_d[ic * 128:(ic + 1) * 128, :], stg, [m_v]))
    m_dq_done = dq_sq[-1]

    ls = LnState()
    ls.m_xt_prev = [dq_v[-1]]
    ls.dep_first_evac = [m_dq_done]
    ps = PsumState()
    ps.wrel = [dq_v[-1]]
    m_proj_stores = []
    for tb in (0, 1):
        m_arena = ln_transpose_block(b, bufs, ls, xb, tb * 1024, arena,
                                     psS, t_eye)
        if tb == 0:
            m_proj_stores.append(project(b, bufs, ps, wqT, arena, qt_d, SO,
                                         True, m_arena))
        m_proj_stores.append(project(b, bufs, ps, wkT, arena, kt_d, SO, True,
                                     m_arena, out_col0=tb * 1024))
        m_proj_stores.append(project(b, bufs, ps, wvT, arena, v_d, SO, False,
                                     m_arena, out_row0=tb * 1024))
        if tb == 0:
            m_proj_stores.append(project(b, bufs, ps, wt_d, arena, f_d, SO,
                                         False, m_arena, dep_w=[m_dq_done]))
        if tb == 0:
            ls.dep_first_evac = [ps.wrel[-1]]

    vr = v_d.rearrange("(kc p) o -> p kc o", p=128)
    m_head_use = None
    m_oevac = []
    first_oevac_dep = ps.wrel[-1]
    for h in range(32):
        w = list(m_proj_stores) if h == 0 else [m_head_use]
        b.dma("sync", kh, kt_d[h * 128:(h + 1) * 128, :], w)
        b.dma("sync", qh, qt_d[h * 128:(h + 1) * 128, :])
        b.dma("sync", vh[:, 0:8 * 128].rearrange("p (c o) -> p c o", c=8),
              vr[:, 0:8, h * 128:(h + 1) * 128])
        m_hl = b.dma("sync", vh[:, 8 * 128:16 * 128].rearrange(
            "p (c o) -> p c o", c=8), vr[:, 8:16, h * 128:(h + 1) * 128])
        for qt in range(2):
            m_exp = {}
            m_zav = {}
            for kc in range(16):
                w = [m_hl]
                if kc >= 2:
                    w.append(m_exp[kc - 2])
                if kc == 0 and m_oevac:
                    w.append(m_oevac[-1])
                if kc == 0 and len(ls.evacs) >= 1:
                    w.append(ls.evacs[-1])
                m_s = b.op("tensor", lambda e, kc=kc, qt=qt: e.matmul(
                    psS[kc % 2], kh[:, kc * 128:(kc + 1) * 128],
                    qh[:, qt * 512:(qt + 1) * 512], start=True, stop=True), w)
                w = [m_s]
                if kc >= 2:
                    w.append(m_zav[kc - 2])
                m_exp[kc] = b.op("scalar", lambda e, kc=kc: e.activation(
                    pt[kc % 2], psS[kc % 2], AF.Exp, scale=RSQ_HD), w)
                m_zav[kc] = b.op("tensor", lambda e, kc=kc: (
                    e.matmul(psZ, t_ones, pt[kc % 2],
                             start=(kc == 0), stop=(kc == 15)),
                    e.matmul(psO, vh[:, kc * 128:(kc + 1) * 128], pt[kc % 2],
                             start=(kc == 0), stop=(kc == 15)))[-1],
                    [m_exp[kc], m_ones])
            m_rz = b.op("vector", lambda e: e.reciprocal(rz, psZ), [m_zav[15]])
            m_bc = b.op("tensor", lambda e: e.matmul(
                psB, t_onesr, rz, start=True, stop=True), [m_rz, m_onesr])
            m_cb = b.op("vector", lambda e: e.tensor_copy(rzb, psB), [m_bc])
            m_ev = b.op("vector", lambda e, h=h, qt=qt: e.tensor_tensor(
                arena[:, h * SO + qt * 512: h * SO + (qt + 1) * 512],
                psO, rzb, ALU.mult), [m_cb, first_oevac_dep])
            first_oevac_dep = None
            m_oevac.append(m_ev)
            m_head_use = m_zav[15]

    wr = woT.rearrange("(c p) o -> p c o", p=128)
    wb, ost = bufs["wb"], bufs["ost"]
    m_sl_use = m_oevac[-1]
    m_store_marks = []
    for ob in range(16):
        w = [ps.wrel[-1]]
        dst = wb.rearrange("p (c o) -> p c o", c=32)
        b.dma("sync", dst[:, 0:16, :], wr[:, 0:16, ob * 256:(ob + 1) * 256], w)
        m_wload = b.dma("sync", dst[:, 16:32, :],
                        wr[:, 16:32, ob * 256:(ob + 1) * 256])
        m_fbb = b.dma("sync", fbb,
                      fbias[:, ob * 256:(ob + 1) * 256].partition_broadcast(128),
                      [m_sl_use])
        m_mm = None
        for tc in range(8):
            m_x = b.dma("sync", xsl,
                        xb[tc * 128:(tc + 1) * 128, ob * 256:(ob + 1) * 256],
                        [m_sl_use])
            m_f = b.dma("sync", fsl,
                        f_d[tc * 128:(tc + 1) * 128, ob * 256:(ob + 1) * 256])
            w = [m_wload, m_oevac[-1]]
            if len(ps.ev) >= 2:
                w.append(ps.ev[-2])
            ci = len(ps.ev)

            def mms(e, ob=ob, tc=tc, ci=ci):
                for ic in range(32):
                    last = e.matmul(
                        psA[ci % 2][:, 0:256],
                        arena[:, ic * SO + tc * 128: ic * SO + (tc + 1) * 128],
                        wb[:, ic * 256:(ic + 1) * 256],
                        start=(ic == 0), stop=(ic == 31))
                return last

            m_mm = b.op("tensor", mms, w)
            w = [m_mm, m_f, m_fbb]
            if len(ps.sto) >= 2:
                w.append(ps.sto[-2])
            m_ev = b.op("vector", lambda e, ob=ob, ci=ci: (
                e.tensor_tensor(ost[ci % 2][:, 0:256],
                                psA[ci % 2][:, 0:256], xsl, ALU.add),
                e.tensor_tensor(ost[ci % 2][:, 0:256],
                                ost[ci % 2][:, 0:256], fsl, ALU.add),
                e.tensor_tensor(ost[ci % 2][:, 0:256],
                                ost[ci % 2][:, 0:256], fbb, ALU.add))[-1],
                w)
            ps.ev.append(m_ev)
            m_sl_use = m_ev
            m_st = b.dma("scalar",
                         x1[tc * 128:(tc + 1) * 128, ob * 256:(ob + 1) * 256],
                         ost[ci % 2][:, 0:256], [m_ev])
            ps.sto.append(m_st)
            m_store_marks.append(m_st)
        ps.wrel.append(m_mm)

    b.wait_only("gpsimd", [m_store_marks[-1]])
    return make_sems_and_emit(nc, b)


def prep_inputs_a(inputs, b_idx, g):
    x = np.asarray(inputs["L0_attn_x"])
    xb = x[b_idx]
    if g == 1:
        xb = np.concatenate([xb[1024:], xb[:1024]], axis=0)
    q = np.asarray(inputs["L0_ffn_qweight"]).reshape(D, D)
    return dict(
        xb=np.ascontiguousarray(xb, dtype=np.float32),
        wqT=np.ascontiguousarray(np.asarray(inputs["L0_attn_Wq"]).T, np.float32),
        wkT=np.ascontiguousarray(np.asarray(inputs["L0_attn_Wk"]).T, np.float32),
        wvT=np.ascontiguousarray(np.asarray(inputs["L0_attn_Wv"]).T, np.float32),
        woT=np.ascontiguousarray(np.asarray(inputs["L0_attn_Wo"]).T, np.float32),
        qTi=np.ascontiguousarray(q.T, np.int32),
        scT=np.ascontiguousarray(np.asarray(inputs["L0_ffn_scales"])[:, :, 0].T,
                                 np.float32),
        szT=np.ascontiguousarray(
            np.asarray(inputs["L0_ffn_scaled_zeros"])[:, :, 0].T, np.float32),
        fbias=np.ascontiguousarray(
            np.asarray(inputs["L0_ffn_bias"]).reshape(1, D), np.float32),
        eye=np.eye(128, dtype=np.float32),
        onesc=np.ones((128, 1), dtype=np.float32),
        onesr=np.ones((1, 128), dtype=np.float32),
    )


NIG = 86
SGSZ = 4


def ln_store(b, bufs, src_dram, n_tiles, dst_dram, ls):
    xt, xnt = bufs["xt"], bufs["xnt"]
    m_last = None
    for tt in range(n_tiles):
        m_ld = b.dma("sync", xt[0], src_dram[tt * 128:(tt + 1) * 128, :],
                     [ls.m_xt_prev[0]])
        m_var, m_norm = ln_tile(b, bufs, xt[0], xnt,
                                [m_ld, ls.m_var_prev, ls.m_xnt_prev], [], D)
        ls.m_var_prev = m_var
        ls.m_xt_prev[0] = m_norm
        m_last = b.dma("scalar", dst_dram[tt * 128:(tt + 1) * 128, :], xnt,
                       [m_norm])
        ls.m_xt_prev[0] = m_last
        ls.m_xnt_prev = m_last
    return m_last


def transpose_fill(b, bufs, ls, xn_dram, row0, n_tiles, ic0, n_ic, arena,
                   psT, t_eye, dep=()):
    xt = bufs["xt"]
    last = None
    dep = list(dep)
    m_tr = None
    for tt in range(n_tiles):
        m_ld = b.dma("sync", xt[0][:, 0:n_ic * 128],
                     xn_dram[row0 + tt * 128: row0 + (tt + 1) * 128,
                             ic0 * 128:(ic0 + n_ic) * 128],
                     [ls.m_xt_prev[0]] + dep)
        dep = []
        for j in range(n_ic):
            w = [m_ld]
            if len(ls.evacs) >= 2:
                w.append(ls.evacs[-2])
            m_tr = b.op("tensor", lambda e, j=j: e.transpose(
                psT[j % 2][:, 0:128], xt[0][:, j * 128:(j + 1) * 128], t_eye), w)
            w2 = [m_tr] + ls.dep_first_evac
            ls.dep_first_evac = []
            last = b.op("vector", lambda e, j=j, tt=tt: e.tensor_copy(
                arena[:, j * SO + tt * 128: j * SO + (tt + 1) * 128],
                psT[j % 2][:, 0:128]), w2)
            ls.evacs.append(last)
        ls.m_xt_prev[0] = m_tr
    return last


def mini_ln(b, t_ones, hT, yn_dst, Pd, psStat, wv, bv, dep):
    sq4, mrow, srow = Pd["sq4"], Pd["mrow"], Pd["srow"]
    mub, rb, brow_d = Pd["mub"], Pd["rb"], Pd["brow_d"]
    dep = list(dep)
    m_mu = []
    for qt in range(4):
        w = list(dep)
        if len(m_mu) >= 2:
            w.append(m_mu[-2])
        m_mm = b.op("tensor", lambda e, qt=qt: [e.matmul(
            psStat[qt % 2], t_ones,
            hT[:, c * 2048 + qt * 512: c * 2048 + (qt + 1) * 512],
            start=(c == 0), stop=(c == 3)) for c in range(4)][-1], w)
        m_mu.append(b.op("vector", lambda e, qt=qt: e.tensor_copy(
            mrow[:, qt * 512:(qt + 1) * 512], psStat[qt % 2]), [m_mm]))
    m_sq4 = []
    for c in range(4):
        m_sq4.append(b.op("scalar", lambda e, c=c: e.activation(
            sq4[:, c * 2048:(c + 1) * 2048], hT[:, c * 2048:(c + 1) * 2048],
            AF.Square), dep))
    m_s = []
    for qt in range(4):
        w = [m_sq4[-1]]
        if len(m_s) >= 2:
            w.append(m_s[-2])
        m_mm = b.op("tensor", lambda e, qt=qt: [e.matmul(
            psStat[qt % 2], t_ones,
            sq4[:, c * 2048 + qt * 512: c * 2048 + (qt + 1) * 512],
            start=(c == 0), stop=(c == 3)) for c in range(4)][-1], w)
        m_s.append(b.op("vector", lambda e, qt=qt: e.tensor_copy(
            srow[:, qt * 512:(qt + 1) * 512], psStat[qt % 2]), [m_mm]))
    m_stat = b.op("vector", lambda e: (
        e.tensor_scalar(mrow, mrow, 1.0 / 512, None, ALU.mult),
        e.tensor_scalar(srow, srow, 1.0 / 512, EPS, ALU.mult, ALU.add),
        e.tensor_tensor(sq4[0:1, 0:2048], mrow, mrow, ALU.mult),
        e.tensor_tensor(srow, srow, sq4[0:1, 0:2048], ALU.subtract))[-1],
        [m_mu[-1], m_s[-1]])
    m_sqrt = b.op("scalar", lambda e: e.activation(sq4[0:1, 0:2048], srow,
                                                   AF.Sqrt), [m_stat])
    m_r = b.op("vector", lambda e: (
        e.reciprocal(sq4[0:1, 2048:4096], sq4[0:1, 0:2048]),
        e.tensor_copy(sq4[0:1, 4096:6144], mrow),
        e.tensor_tensor(sq4[0:1, 6144:8192], sq4[0:1, 2048:4096],
                        sq4[0:1, 2048:4096], ALU.mult),
        e.tensor_tensor(sq4[0:1, 6144:8192], sq4[0:1, 6144:8192], srow,
                        ALU.mult),
        e.tensor_scalar(sq4[0:1, 6144:8192], sq4[0:1, 6144:8192], -0.5, 1.5,
                        ALU.mult, ALU.add),
        e.tensor_tensor(sq4[0:1, 2048:4096], sq4[0:1, 2048:4096],
                        sq4[0:1, 6144:8192], ALU.mult))[-1], [m_sqrt])
    m_w1 = b.dma("scalar", brow_d[0:1, :], mrow, [m_r])
    m_w2 = b.dma("scalar", brow_d[1:2, :], sq4[0:1, 2048:4096], [m_r])
    m_b1 = b.dma("sync", mub, brow_d[0:1, :].partition_broadcast(128), [m_w1])
    m_b2 = b.dma("sync", rb, brow_d[1:2, :].partition_broadcast(128), [m_w2])
    yn_marks = []
    for c in range(4):
        yn_marks.append(b.op("vector", lambda e, c=c: (
            e.tensor_tensor(yn_dst(c), hT[:, c * 2048:(c + 1) * 2048], mub,
                            ALU.subtract),
            e.tensor_tensor(yn_dst(c), yn_dst(c), rb, ALU.mult),
            e.tensor_scalar(yn_dst(c), yn_dst(c), wv(c), bv(c),
                            ALU.mult, ALU.add))[-1], [m_b1, m_b2]))
    return yn_marks


def emit_scan(b, idx, m_y, planes, ckr, cki, ncki, t_ilr, t_ili, zeros_ap,
              dep_planes, dump=None):
    Ar, Ai, Br, Bi = planes
    m_init = b.op("vector", lambda e: (
        e.tensor_copy(Ai, zeros_ap),
        e.tensor_tensor(Ar[:, 0:1], Ar[:, 0:1], t_ilr[:, idx:idx + 1],
                        ALU.add),
        e.tensor_copy(Ai[:, 0:1], t_ili[:, idx:idx + 1]))[-1],
        [m_y] + list(dep_planes))
    cur_r, cur_i, nxt_r, nxt_i = Ar, Ai, Br, Bi
    m = m_init
    if dump is not None:
        m = b.dma("scalar", dump[0], Ar, [m_init])
        m = b.dma("scalar", dump[1], Ai, [m])
    for k in range(11):
        sft = 1 << k
        cr = ckr[:, k * 16 + idx: k * 16 + idx + 1]
        ci = cki[:, k * 16 + idx: k * 16 + idx + 1]
        nci = ncki[:, k * 16 + idx: k * 16 + idx + 1]

        def lvl(e, sft=sft, cr=cr, ci=ci, nci=nci, cur_r=cur_r, cur_i=cur_i,
                nxt_r=nxt_r, nxt_i=nxt_i):
            n = S - sft
            e.scalar_tensor_tensor(nxt_r[:, sft:S], cur_r[:, 0:n], cr,
                                   cur_r[:, sft:S], ALU.mult, ALU.add)
            e.scalar_tensor_tensor(nxt_i[:, sft:S], cur_i[:, 0:n], cr,
                                   cur_i[:, sft:S], ALU.mult, ALU.add)
            e.scalar_tensor_tensor(nxt_r[:, sft:S], cur_i[:, 0:n], nci,
                                   nxt_r[:, sft:S], ALU.mult, ALU.add)
            e.scalar_tensor_tensor(nxt_i[:, sft:S], cur_r[:, 0:n], ci,
                                   nxt_i[:, sft:S], ALU.mult, ALU.add)
            e.tensor_copy(nxt_r[:, 0:sft], cur_r[:, 0:sft])
            return e.tensor_copy(nxt_i[:, 0:sft], cur_i[:, 0:sft])

        m = b.op("vector", lvl, [m])
        cur_r, cur_i, nxt_r, nxt_i = nxt_r, nxt_i, cur_r, cur_i
    if dump is not None:
        m = b.dma("scalar", dump[2], cur_r, [m])
    return cur_r, m


def build_launch_b(debug=False):
    nc = bass.Bass("TRN2", target_bir_lowering=False)
    x1b = nc.dram_tensor("x1b", [S, D], F32, kind="ExternalInput").ap()
    x1own = nc.dram_tensor("x1own", [SO, D], F32, kind="ExternalInput").ap()
    winT = nc.dram_tensor("winT", [D, 512], F32, kind="ExternalInput").ap()
    w1T = nc.dram_tensor("w1T", [4, 512, 2048], F32, kind="ExternalInput").ap()
    w2T = nc.dram_tensor("w2T", [4, 2048, 512], F32, kind="ExternalInput").ap()
    woutT = nc.dram_tensor("woutT", [512, D], F32, kind="ExternalInput").ap()
    U1 = nc.dram_tensor("U1", [D, 1024], F32, kind="ExternalInput").ap()
    V1 = nc.dram_tensor("V1", [1024, 22016], F32, kind="ExternalInput").ap()
    U2 = nc.dram_tensor("U2", [11008, 1024], F32, kind="ExternalInput").ap()
    V2 = nc.dram_tensor("V2", [1024, D], F32, kind="ExternalInput").ap()
    pbin = nc.dram_tensor("pbin", [128, 4], F32, kind="ExternalInput").ap()
    lnw1 = nc.dram_tensor("lnw1", [128, 16], F32, kind="ExternalInput").ap()
    lnb1 = nc.dram_tensor("lnb1", [128, 16], F32, kind="ExternalInput").ap()
    lnw2 = nc.dram_tensor("lnw2", [128, 16], F32, kind="ExternalInput").ap()
    lnb2 = nc.dram_tensor("lnb2", [128, 16], F32, kind="ExternalInput").ap()
    lnlw = nc.dram_tensor("lnlw", [128, 4], F32, kind="ExternalInput").ap()
    lnlb = nc.dram_tensor("lnlb", [128, 4], F32, kind="ExternalInput").ap()
    prz = nc.dram_tensor("prz", [128, 16], F32, kind="ExternalInput").ap()
    piz = nc.dram_tensor("piz", [128, 16], F32, kind="ExternalInput").ap()
    lcr = nc.dram_tensor("lcr", [128, 16], F32, kind="ExternalInput").ap()
    lci = nc.dram_tensor("lci", [128, 16], F32, kind="ExternalInput").ap()
    sclz = nc.dram_tensor("sclz", [128, 16], F32, kind="ExternalInput").ap()
    b1m = nc.dram_tensor("b1m", [128, 64], F32, kind="ExternalInput").ap()
    b2m = nc.dram_tensor("b2m", [128, 16], F32, kind="ExternalInput").ap()
    b1f = nc.dram_tensor("b1f", [128, 172], F32, kind="ExternalInput").ap()
    b2f = nc.dram_tensor("b2f", [1, D], F32, kind="ExternalInput").ap()
    bout = nc.dram_tensor("bout", [1, D], F32, kind="ExternalInput").ap()
    eye = nc.dram_tensor("eye", [128, 128], F32, kind="ExternalInput").ap()
    onesc = nc.dram_tensor("onesc", [128, 1], F32, kind="ExternalInput").ap()
    zrow = nc.dram_tensor("zrow", [128, S], F32, kind="ExternalInput").ap()

    pa = nc.dram_tensor("pa", [SO, D], F32, kind="ExternalOutput").ap()
    atnf = nc.dram_tensor("atnf", [S, D], F32, kind="ExternalOutput").ap()
    dbg = "ExternalOutput" if debug else "Internal"
    xn1_d = nc.dram_tensor("xn1_d", [S, D], F32, kind=dbg).ap()
    xno_d = nc.dram_tensor("xno_d", [SO, D], F32, kind=dbg).ap()
    ht_d = nc.dram_tensor("ht_d", [128, 4 * S], F32, kind=dbg).ap()
    yn_d = nc.dram_tensor("yn_d", [128, 4 * S], F32, kind=dbg).ap()
    st_d = nc.dram_tensor("st_d", [4, S], F32, kind=dbg).ap()
    pw_d = nc.dram_tensor("pw_d", [128, 600], F32, kind=dbg).ap()
    sc_d = nc.dram_tensor("sc_d", [3, 128, S], F32, kind=dbg).ap()
    cv_d = nc.dram_tensor("cv_d", [128, 4 * S], F32, kind=dbg).ap()
    hu_d = nc.dram_tensor("hu_d", [128, 4 * S], F32, kind=dbg).ap()
    pt_d = nc.dram_tensor("pt_d", [128, 4 * S], F32, kind=dbg).ap()
    z_d = nc.dram_tensor("z_d", [128, 4 * S], F32, kind=dbg).ap()
    h2_d = nc.dram_tensor("h2_d", [128, 4 * S], F32, kind=dbg).ap()
    brow_d = nc.dram_tensor("brow_d", [2, S], F32).ap()

    arena = nc.alloc_sbuf_tensor("arena", [128, 8 * SO], F32).ap()
    hT = nc.alloc_sbuf_tensor("hT", [128, 4 * S], F32).ap()
    wk = nc.alloc_sbuf_tensor("wk", [128, 4 * S + 4096 + 1536], F32).ap()
    PT = nc.alloc_sbuf_tensor("PT", [128, 4 * S], F32).ap()
    zb = nc.alloc_sbuf_tensor("zb", [128, 4 * S], F32).ap()
    ost = [nc.alloc_sbuf_tensor(f"ost{i}", [128, 512], F32).ap() for i in (0, 1)]
    stt = {k: nc.alloc_sbuf_tensor("st_" + k, [128, 1], F32).ap()
           for k in ("musum", "sqsum", "mu", "r", "t1", "t2")}
    t_eye = nc.alloc_sbuf_tensor("t_eye", [128, 128], F32).ap()
    t_ones = nc.alloc_sbuf_tensor("t_ones", [128, 1], F32).ap()
    prm = nc.alloc_sbuf_tensor("prm", [128, 1000], F32).ap()
    mrow = zb[0:1, 2560:4608]
    srow = zb[0:1, 4608:6656]
    WS = 2 * D

    def Pr(c0, n):
        return prm[:, c0:c0 + n]
    t_bin = Pr(0, 4)
    t_lnw1, t_lnb1 = Pr(4, 16), Pr(20, 16)
    t_lnw2, t_lnb2 = Pr(36, 16), Pr(52, 16)
    t_lnlw, t_lnlb = Pr(68, 4), Pr(72, 4)
    t_pr, t_pi = Pr(76, 16), Pr(92, 16)
    t_lcr, t_lci = Pr(108, 16), Pr(124, 16)
    t_scl = Pr(140, 16)
    t_b1m, t_b2m = Pr(156, 64), Pr(220, 16)
    t_b1f = Pr(236, 172)
    ckr, cki, ncki = Pr(408, 176), Pr(584, 176), Pr(760, 176)
    t_ilr, t_ili = Pr(936, 16), Pr(952, 16)
    tmp4 = Pr(968, 32)

    psA = [nc.alloc_psum_tensor(f"psA{i}", [128, 512], F32).ap() for i in (0, 1)]
    psT = [nc.alloc_psum_tensor(f"psT{i}", [128, 512], F32).ap() for i in (0, 1)]
    psStat = [nc.alloc_psum_tensor(f"psU{i}", [1, 512], F32).ap() for i in (0, 1)]

    bufs = {"xt": [wk[:, 0:D]], "xnt": wk[:, 0:D], "scr": zb[:, 0:D],
            "st": stt}

    b = B(nc)
    m_eye = b.dma("sync", t_eye, eye)
    m_ones = b.dma("sync", t_ones, onesc)
    m_prm = None
    for (tile, src) in ((t_bin, pbin), (t_lnw1, lnw1), (t_lnb1, lnb1),
                        (t_lnw2, lnw2), (t_lnb2, lnb2), (t_lnlw, lnlw),
                        (t_lnlb, lnlb), (t_pr, prz), (t_pi, piz),
                        (t_lcr, lcr), (t_lci, lci), (t_scl, sclz),
                        (t_b1m, b1m), (t_b2m, b2m), (t_b1f, b1f)):
        m_prm = b.dma("sync", tile, src)

    a4, b4 = tmp4[:, 0:16], tmp4[:, 16:32]
    m_pw0 = b.op("vector", lambda e: (
        e.tensor_tensor(a4, t_pr, t_pr, ALU.mult),
        e.tensor_tensor(b4, t_pi, t_pi, ALU.mult),
        e.tensor_copy(ckr[:, 0:16], t_pr),
        e.tensor_tensor(a4, a4, b4, ALU.add))[-1], [m_prm])
    m_pw1 = b.op("scalar", lambda e: e.activation(b4, a4, AF.Sqrt), [m_pw0])
    m_pw1b = b.op("scalar", lambda e: e.activation(tmp4[:, 0:16], b4,
                                                   AF.Copy), [m_pw1], 0)
    m_pw2 = b.op("scalar", lambda e: e.activation(a4, b4, AF.Tanh), [m_pw1])
    m_pw3 = b.op("vector", lambda e: (
        e.reciprocal(cki[:, 0:16], b4),
        e.tensor_copy(ncki[:, 0:16], t_pr),
        e.tensor_tensor(cki[:, 0:16], cki[:, 0:16], a4, ALU.mult),
        e.tensor_copy(ncki[:, 16:32], t_pi),
        e.tensor_tensor(ckr[:, 0:16], t_pr, cki[:, 0:16], ALU.mult),
        e.tensor_tensor(cki[:, 0:16], t_pi, cki[:, 0:16], ALU.mult))[-1],
        [m_pw2])
    m_pw4 = b.op("vector", lambda e: (
        e.tensor_tensor(a4, ckr[:, 0:16], t_lcr, ALU.mult),
        e.tensor_tensor(b4, cki[:, 0:16], t_lci, ALU.mult),
        e.tensor_tensor(t_ilr, ckr[:, 0:16], t_lci, ALU.mult),
        e.tensor_tensor(t_ili, cki[:, 0:16], t_lcr, ALU.mult),
        e.tensor_tensor(a4, a4, b4, ALU.subtract),
        e.tensor_tensor(t_ili, t_ili, t_ilr, ALU.add),
        e.tensor_copy(t_ilr, a4),
        e.tensor_scalar(ncki[:, 0:16], cki[:, 0:16], -1.0, None, ALU.mult))[-1],
        [m_pw3])
    m_pw = m_pw4
    for k in range(1, 11):
        pc, cc = (k - 1) * 16, k * 16
        m_pw = b.op("vector", lambda e, pc=pc, cc=cc: (
            e.tensor_tensor(a4, ckr[:, pc:pc + 16], ckr[:, pc:pc + 16],
                            ALU.mult),
            e.tensor_tensor(b4, cki[:, pc:pc + 16], cki[:, pc:pc + 16],
                            ALU.mult),
            e.tensor_tensor(cki[:, cc:cc + 16], ckr[:, pc:pc + 16],
                            cki[:, pc:pc + 16], ALU.mult),
            e.tensor_tensor(ckr[:, cc:cc + 16], a4, b4, ALU.subtract),
            e.tensor_scalar(ncki[:, cc:cc + 16], cki[:, cc:cc + 16], -2.0,
                            None, ALU.mult),
            e.tensor_scalar(cki[:, cc:cc + 16], cki[:, cc:cc + 16], 2.0,
                            None, ALU.mult))[-1], [m_pw])
    m_powers = m_pw
    if debug:
        m_powers = b.dma("scalar", pw_d, prm[:, 400:1000], [m_pw])

    ls = LnState()
    m_xn1 = ln_store(b, bufs, x1b, 16, xn1_d, ls)
    m_xno = ln_store(b, bufs, x1own, 8, xno_d, ls)

    ps = PsumState()
    winr = winT.rearrange("(q c p) o -> q p c o", q=4, p=128)
    m_mm = None
    for qtr in range(4):
        for tb in (0, 1):
            if ps.wrel:
                ls.dep_first_evac = [ps.wrel[-1]]
            m_fill = transpose_fill(b, bufs, ls, xn1_d, tb * 1024, 8,
                                    qtr * 8, 8, arena, psT, t_eye,
                                    [m_xn1, m_xno])
            w = [m_xn1]
            if ps.wrel:
                w.append(ps.wrel[-1])
            m_wl = b.dma("sync", wk[:, WS:WS + 8 * 512].rearrange(
                "p (c o) -> p c o", c=8), winr[qtr], w)
            for oc in range(4):
                for qt in range(2):
                    w = [m_wl, m_fill]
                    if len(ps.ev) >= 2:
                        w.append(ps.ev[-2])
                    ci = len(ps.ev)

                    def mms(e, oc=oc, qt=qt, ci=ci):
                        for c in range(8):
                            last = e.matmul(
                                psA[ci % 2],
                                wk[:, WS + c * 512 + oc * 128:
                                   WS + c * 512 + (oc + 1) * 128],
                                arena[:, c * SO + qt * 512:
                                      c * SO + (qt + 1) * 512],
                                start=(c == 0), stop=(c == 7))
                        return last

                    m_mm = b.op("tensor", mms, w)
                    dst = hT[:, oc * 2048 + tb * 1024 + qt * 512:
                             oc * 2048 + tb * 1024 + (qt + 1) * 512]
                    if qtr == 0:
                        m_ev = b.op("vector", lambda e, dst=dst, ci=ci:
                                    e.tensor_copy(dst, psA[ci % 2]), [m_mm])
                    else:
                        m_ev = b.op("vector", lambda e, dst=dst, ci=ci:
                                    e.tensor_tensor(dst, dst, psA[ci % 2],
                                                    ALU.add), [m_mm])
                    ps.ev.append(m_ev)
            ps.wrel.append(m_mm)
    m_hbin = b.op("vector", lambda e: [e.tensor_scalar(
        hT[:, oc * 2048:(oc + 1) * 2048], hT[:, oc * 2048:(oc + 1) * 2048],
        t_bin[:, oc:oc + 1], None, ALU.add) for oc in range(4)][-1],
        [ps.ev[-1], m_prm])

    if debug:
        m_hbin = b.dma("scalar", ht_d, hT, [m_hbin])
    u1r = U1.rearrange("(q c p) r -> q p c r", q=4, p=128)
    for qtr in range(4):
        ls.dep_first_evac = [ps.wrel[-1]]
        m_fill = transpose_fill(b, bufs, ls, xno_d, 0, 8, qtr * 8, 8,
                                arena, psT, t_eye, [m_xno])
        for rc in range(8):
            m_wl = b.dma("sync", wk[:, WS:WS + 8 * 128].rearrange(
                "p (c r) -> p c r", c=8),
                u1r[qtr][:, :, rc * 128:(rc + 1) * 128], [ps.wrel[-1]])
            for qt in range(2):
                w = [m_wl, m_fill]
                if len(ps.ev) >= 2:
                    w.append(ps.ev[-2])
                ci = len(ps.ev)

                def mms(e, qt=qt, ci=ci):
                    for c in range(8):
                        last = e.matmul(
                            psA[ci % 2],
                            wk[:, WS + c * 128:WS + (c + 1) * 128],
                            arena[:, c * SO + qt * 512:
                                  c * SO + (qt + 1) * 512],
                            start=(c == 0), stop=(c == 7))
                    return last

                m_mm = b.op("tensor", mms, w)
                dst = PT[:, rc * 1024 + qt * 512: rc * 1024 + (qt + 1) * 512]
                if qtr == 0:
                    m_ev = b.op("vector", lambda e, dst=dst, ci=ci:
                                e.tensor_copy(dst, psA[ci % 2]), [m_mm])
                else:
                    m_ev = b.op("vector", lambda e, dst=dst, ci=ci:
                                e.tensor_tensor(dst, dst, psA[ci % 2],
                                                ALU.add), [m_mm])
                ps.ev.append(m_ev)
            ps.wrel.append(m_mm)
    m_PT = ps.ev[-1]

    if debug:
        m_PT = b.dma("scalar", pt_d, PT, [m_PT])
    v1r = V1.rearrange("(c p) f -> p c f", p=128)
    m_z_last = None
    m_arena_free = ps.wrel[-1]
    for sg0 in range(0, NIG, SGSZ):
        igs = list(range(sg0, min(NIG, sg0 + SGSZ)))
        m_act = []
        m_u2l = None
        for s, ig in enumerate(igs):
            sl = s % 2
            w = [m_PT if m_z_last is None else m_z_last]
            if len(m_act) >= 2:
                w.append(m_act[-2])
            m_g = b.dma("sync", wk[:, WS + sl * 2048:
                                   WS + sl * 2048 + 1024].rearrange(
                "p (c f) -> p c f", c=8),
                v1r[:, :, ig * 128:(ig + 1) * 128], w)
            m_u = b.dma("sync", wk[:, WS + sl * 2048 + 1024:
                                   WS + sl * 2048 + 2048].rearrange(
                "p (c f) -> p c f", c=8),
                v1r[:, :, 11008 + ig * 128: 11008 + (ig + 1) * 128])
            m_u2l = b.dma("sync", arena[:, (4 + s) * SO:(4 + s + 1) * SO],
                          U2[ig * 128:(ig + 1) * 128, :], [m_arena_free])
            for qt in range(2):
                cig = len(ps.ev)
                w = [m_g, m_u, m_PT]
                if len(ps.ev) >= 2:
                    w.append(ps.ev[-2])

                def mmg(e, sl=sl, qt=qt, cig=cig):
                    for rc in range(8):
                        last = e.matmul(
                            psA[cig % 2],
                            wk[:, WS + sl * 2048 + rc * 128:
                               WS + sl * 2048 + (rc + 1) * 128],
                            PT[:, rc * 1024 + qt * 512:
                               rc * 1024 + (qt + 1) * 512],
                            start=(rc == 0), stop=(rc == 7))
                    return last

                m_mmg = b.op("tensor", mmg, w)
                m_sg = b.op("scalar", lambda e, ig=ig, cig=cig: e.activation(
                    ost[0], psA[cig % 2], AF.Silu,
                    bias=t_b1f[:, ig:ig + 1]), [m_mmg])
                ps.ev.append(m_sg)
                ciu = len(ps.ev)
                w = [m_sg]
                if len(ps.ev) >= 2:
                    w.append(ps.ev[-2])

                def mmu(e, sl=sl, qt=qt, ciu=ciu):
                    for rc in range(8):
                        last = e.matmul(
                            psA[ciu % 2],
                            wk[:, WS + sl * 2048 + 1024 + rc * 128:
                               WS + sl * 2048 + 1024 + (rc + 1) * 128],
                            PT[:, rc * 1024 + qt * 512:
                               rc * 1024 + (qt + 1) * 512],
                            start=(rc == 0), stop=(rc == 7))
                    return last

                m_mmu = b.op("tensor", mmu, w)
                m_su = b.op("scalar", lambda e, ig=ig, ciu=ciu: e.activation(
                    ost[1], psA[ciu % 2], AF.Identity,
                    bias=t_b1f[:, 86 + ig:87 + ig]), [m_mmu])
                ps.ev.append(m_su)
                m_a = b.op("vector", lambda e, s=s, qt=qt: e.tensor_tensor(
                    arena[:, s * SO + qt * 512: s * SO + (qt + 1) * 512],
                    ost[0], ost[1], ALU.mult), [m_su, m_arena_free])
                m_act.append(m_a)
        for rc in range(8):
            for qt in range(2):
                ciz = len(ps.ev)
                w = [m_act[-1], m_u2l]
                if len(ps.ev) >= 2:
                    w.append(ps.ev[-2])

                def mmz(e, rc=rc, qt=qt, ciz=ciz, nig=len(igs)):
                    for s in range(nig):
                        last = e.matmul(
                            psA[ciz % 2],
                            arena[:, (4 + s) * SO + rc * 128:
                                  (4 + s) * SO + (rc + 1) * 128],
                            arena[:, s * SO + qt * 512:
                                  s * SO + (qt + 1) * 512],
                            start=(s == 0), stop=(s == nig - 1))
                    return last

                m_mmz = b.op("tensor", mmz, w)
                dst = zb[:, rc * 1024 + qt * 512: rc * 1024 + (qt + 1) * 512]
                if sg0 == 0:
                    m_ev = b.op("vector", lambda e, dst=dst, ciz=ciz:
                                e.tensor_copy(dst, psA[ciz % 2]), [m_mmz])
                else:
                    m_ev = b.op("vector", lambda e, dst=dst, ciz=ciz:
                                e.tensor_tensor(dst, dst, psA[ciz % 2],
                                                ALU.add), [m_mmz])
                ps.ev.append(m_ev)
                m_z_last = m_ev
        m_arena_free = m_z_last

    if debug:
        m_z_last = b.dma("scalar", z_d, zb, [m_z_last])
    v2r = V2.rearrange("(c p) o -> p c o", p=128)
    xsl = wk[:, WS + 4096:WS + 4096 + 512]
    bfb = wk[:, WS + 4608:WS + 4608 + 512]
    m_sl_use = m_z_last
    m_pa_stores = []
    for ob in range(8):
        m_wl = b.dma("sync", wk[:, WS:WS + 8 * 512].rearrange(
            "p (c o) -> p c o", c=8), v2r[:, :, ob * 512:(ob + 1) * 512],
            [ps.wrel[-1], m_z_last])
        m_bf = b.dma("sync", bfb,
                     b2f[:, ob * 512:(ob + 1) * 512].partition_broadcast(128),
                     [m_sl_use])
        for tt in range(8):
            m_x = b.dma("sync", xsl,
                        x1own[tt * 128:(tt + 1) * 128,
                              ob * 512:(ob + 1) * 512], [m_sl_use])
            ci = len(ps.ev)
            w = [m_wl, m_x, m_z_last]
            if len(ps.ev) >= 2:
                w.append(ps.ev[-2])

            def mmv(e, ob=ob, tt=tt, ci=ci):
                for rc in range(8):
                    last = e.matmul(
                        psA[ci % 2],
                        zb[:, rc * 1024 + tt * 128: rc * 1024 + (tt + 1) * 128],
                        wk[:, WS + rc * 512:WS + (rc + 1) * 512],
                        start=(rc == 0), stop=(rc == 7))
                return last

            m_mm = b.op("tensor", mmv, w)
            w = [m_mm, m_bf]
            if len(ps.sto) >= 2:
                w.append(ps.sto[-2])
            m_ev = b.op("vector", lambda e, ci=ci: (
                e.tensor_tensor(ost[ci % 2], psA[ci % 2], xsl, ALU.add),
                e.tensor_tensor(ost[ci % 2], ost[ci % 2], bfb, ALU.add))[-1],
                w)
            ps.ev.append(m_ev)
            m_sl_use = m_ev
            m_st = b.dma("scalar",
                         pa[tt * 128:(tt + 1) * 128, ob * 512:(ob + 1) * 512],
                         ost[ci % 2], [m_ev])
            ps.sto.append(m_st)
            m_pa_stores.append(m_st)
        ps.wrel.append(m_mm)

    b.serial = True
    Pd = {"sq4": PT, "mrow": mrow, "srow": srow,
          "mub": PT[:, 0:2048], "rb": PT[:, 2048:4096], "brow_d": brow_d}
    zeros_ap = zb[:, 0:2048]
    scan_planes = [wk[:, 0:2048], wk[:, 2048:4096], wk[:, 4096:6144],
                   wk[:, 6144:8192]]
    m_h = [m_hbin]
    m_wk_free = m_pa_stores[-1]
    m_arena_b2 = m_z_last

    for l in range(4):
        m_zeros = b.dma("sync", zb[:, 0:2048], zrow[:, 0:2048],
                        m_h + ([m_pa_stores[-1]] if l == 0 else []))
        yn1 = mini_ln(b, t_ones, hT, lambda c: arena[:, c * 2048:(c + 1) * 2048],
                      Pd, psStat,
                      lambda c: t_lnw1[:, l * 4 + c:l * 4 + c + 1],
                      lambda c: t_lnb1[:, l * 4 + c:l * 4 + c + 1],
                      m_h + ([m_arena_b2, m_powers] if l == 0 else []))
        if debug and l == 0:
            b.dma("scalar", st_d[0:1, :], Pd["mrow"], [yn1[-1]])
            b.dma("scalar", st_d[1:2, :], Pd["srow"], [yn1[-1]])
            b.dma("scalar", st_d[2:3, :], Pd["mub"][0:1, :], [yn1[-1]])
            yn1[-1] = b.dma("scalar", st_d[3:4, :], Pd["rb"][0:1, :], [yn1[-1]])
        m_arena_b2 = None
        m_upd = []
        m_plane_free = [m_wk_free]
        for c in range(4):
            idx = l * 4 + c
            if debug and l == 0:
                yn1[c] = b.dma("scalar", yn_d[:, c * 2048:(c + 1) * 2048],
                               arena[:, c * 2048:(c + 1) * 2048], [yn1[c]])
            m_silu = b.op("scalar", lambda e, c=c: e.activation(
                scan_planes[0], arena[:, c * 2048:(c + 1) * 2048], AF.Silu),
                [yn1[c]] + m_plane_free)
            conv_r, m_scan = emit_scan(
                b, idx, m_silu, scan_planes, ckr, cki, ncki, t_ilr, t_ili,
                zeros_ap, [m_zeros],
                dump=([sc_d[0], sc_d[1], sc_d[2]]
                      if (debug and l == 0 and c == 0) else None))
            m_plane_free = [m_scan]
            if debug and l == 0:
                m_scan = b.dma("scalar", cv_d[:, c * 2048:(c + 1) * 2048],
                               conv_r, [m_scan])
            m_u = b.op("vector", lambda e, c=c, conv_r=conv_r, idx=idx:
                       e.scalar_tensor_tensor(
                           hT[:, c * 2048:(c + 1) * 2048], conv_r,
                           t_scl[:, idx:idx + 1],
                           hT[:, c * 2048:(c + 1) * 2048],
                           ALU.mult, ALU.add), [m_scan])
            m_upd.append(m_u)
        if debug and l == 0:
            m_upd[-1] = b.dma("scalar", hu_d, hT, [m_upd[-1]])
        yn2 = mini_ln(b, t_ones, hT, lambda c: arena[:, c * 2048:(c + 1) * 2048],
                      Pd, psStat,
                      lambda c: t_lnw2[:, l * 4 + c:l * 4 + c + 1],
                      lambda c: t_lnb2[:, l * 4 + c:l * 4 + c + 1],
                      m_upd)
        m_w1l = b.dma("sync", PT.rearrange("p (c f) -> p c f", c=4),
                      w1T[l].rearrange("(c p) f -> p c f", p=128), [yn2[-1]])
        m_w2l = b.dma("sync", wk[:, 0:8192].rearrange("p (c o) -> p c o", c=16),
                      w2T[l].rearrange("(c p) o -> p c o", p=128),
                      [m_upd[-1]])
        m_hffn = []
        for qt in range(4):
            m_aq = []
            for fc in range(16):
                ci = len(ps.ev)
                w = [m_w1l, yn2[-1]]
                if len(ps.ev) >= 2:
                    w.append(ps.ev[-2])

                def mmf(e, qt=qt, fc=fc, ci=ci):
                    for dc in range(4):
                        last = e.matmul(
                            psA[ci % 2],
                            PT[:, dc * 2048 + fc * 128:
                               dc * 2048 + (fc + 1) * 128],
                            arena[:, dc * 2048 + qt * 512:
                                  dc * 2048 + (qt + 1) * 512],
                            start=(dc == 0), stop=(dc == 3))
                    return last

                m_mm = b.op("tensor", mmf, w)
                m_sa = b.op("scalar", lambda e, l=l, fc=fc, qt=qt, ci=ci:
                            e.activation(
                                zb[:, fc * 512:(fc + 1) * 512],
                                psA[ci % 2], AF.Silu,
                                bias=t_b1m[:, l * 16 + fc:l * 16 + fc + 1]),
                            [m_mm])
                ps.ev.append(m_sa)
                m_aq.append(m_sa)
            for oc in range(4):
                ci = len(ps.ev)
                w = [m_aq[-1], m_w2l]
                if len(ps.ev) >= 2:
                    w.append(ps.ev[-2])

                def mmo(e, qt=qt, oc=oc, ci=ci):
                    for fc in range(16):
                        last = e.matmul(
                            psA[ci % 2],
                            wk[:, fc * 512 + oc * 128:
                               fc * 512 + (oc + 1) * 128],
                            zb[:, fc * 512:(fc + 1) * 512],
                            start=(fc == 0), stop=(fc == 15))
                    return last

                m_mm = b.op("tensor", mmo, w)
                w = [m_mm]
                if len(ps.sto) >= 2:
                    w.append(ps.sto[-2])
                ko = len(ps.sto)
                m_cb = b.op("scalar", lambda e, l=l, oc=oc, ci=ci, ko=ko:
                            e.activation(ost[ko % 2], psA[ci % 2], AF.Identity,
                                         bias=t_b2m[:, l * 4 + oc:
                                                    l * 4 + oc + 1]), w)
                ps.sto.append(m_cb)
                m_hu = b.op("vector", lambda e, oc=oc, qt=qt, ko=ko:
                            e.tensor_tensor(
                                hT[:, oc * 2048 + qt * 512:
                                   oc * 2048 + (qt + 1) * 512],
                                hT[:, oc * 2048 + qt * 512:
                                   oc * 2048 + (qt + 1) * 512],
                                ost[ko % 2], ALU.add), [m_cb])
                ps.ev.append(m_hu)
                m_hffn.append(m_hu)
        m_h = m_hffn
        m_wk_free = m_hffn[-1]

    if debug:
        m_h = [b.dma("scalar", h2_d, hT, m_h)]
    ynl = mini_ln(b, t_ones, hT, lambda c: arena[:, c * 2048:(c + 1) * 2048],
                  Pd, psStat,
                  lambda c: t_lnlw[:, c:c + 1],
                  lambda c: t_lnlb[:, c:c + 1], m_h)
    wor = woutT.rearrange("(c p) o -> p c o", p=128)
    boutb = zb[:, 8192 - 512:8192]
    m_atnf_stores = []
    for ob in range(8):
        m_wl = b.dma("sync", wk[:, 0:4 * 512].rearrange("p (c o) -> p c o", c=4),
                     wor[:, :, ob * 512:(ob + 1) * 512],
                     [ps.wrel[-1], m_h[-1]])
        m_bo = b.dma("sync", boutb,
                     bout[:, ob * 512:(ob + 1) * 512].partition_broadcast(128),
                     [m_h[-1]])
        m_mm = None
        for tt in range(16):
            ci = len(ps.ev)
            w = [m_wl, m_bo, ynl[-1]]
            if len(ps.ev) >= 2:
                w.append(ps.ev[-2])

            def mmw(e, ob=ob, tt=tt, ci=ci):
                for dc in range(4):
                    last = e.matmul(
                        psA[ci % 2],
                        arena[:, dc * 2048 + tt * 128:
                              dc * 2048 + (tt + 1) * 128],
                        wk[:, dc * 512:(dc + 1) * 512],
                        start=(dc == 0), stop=(dc == 3))
                return last

            m_mm = b.op("tensor", mmw, w)
            w = [m_mm]
            if len(ps.sto) >= 2:
                w.append(ps.sto[-2])
            ko = len(ps.sto)
            m_ev = b.op("vector", lambda e, ci=ci, ko=ko: e.tensor_tensor(
                ost[ko % 2], psA[ci % 2], boutb, ALU.add), w)
            ps.ev.append(m_ev)
            m_st = b.dma("scalar",
                         atnf[tt * 128:(tt + 1) * 128, ob * 512:(ob + 1) * 512],
                         ost[ko % 2], [m_ev])
            ps.sto.append(m_st)
            m_atnf_stores.append(m_st)
        ps.wrel.append(m_mm)

    b.serial = False
    b.wait_only("gpsimd", [m_pa_stores[-1], m_atnf_stores[-1]])
    return make_sems_and_emit(nc, b)


def build_launch_c():
    nc = bass.Bass("TRN2", target_bir_lowering=False)
    ia = nc.dram_tensor("ia", [SO, D], F32, kind="ExternalInput").ap()
    ib = nc.dram_tensor("ib", [SO, D], F32, kind="ExternalInput").ap()
    x2 = nc.dram_tensor("x2", [SO, D], F32, kind="ExternalOutput").ap()
    ta = [nc.alloc_sbuf_tensor(f"ta{i}", [128, D], F32).ap() for i in (0, 1)]
    tb = [nc.alloc_sbuf_tensor(f"tb{i}", [128, D], F32).ap() for i in (0, 1)]
    b = B(nc)
    stores = []
    for tt in range(8):
        sl = tt % 2
        w = [stores[tt - 2]] if tt >= 2 else []
        b.dma("sync", ta[sl], ia[tt * 128:(tt + 1) * 128, :], w)
        m_l = b.dma("sync", tb[sl], ib[tt * 128:(tt + 1) * 128, :])
        m_v = b.op("vector", lambda e, sl=sl: e.tensor_tensor(
            ta[sl], ta[sl], tb[sl], ALU.add), [m_l])
        stores.append(b.dma("scalar", x2[tt * 128:(tt + 1) * 128, :],
                            ta[sl], [m_v]))
    b.wait_only("gpsimd", stores)
    return make_sems_and_emit(nc, b)


def prep_inputs_b(inputs, x1, b_idx, g):
    def T(a):
        return np.ascontiguousarray(np.asarray(a).T, np.float32)

    def cols16(a):
        return np.ascontiguousarray(
            np.asarray(a, np.float32).reshape(4, 4, 128).transpose(2, 0, 1)
            .reshape(128, 16))

    x1b = np.ascontiguousarray(x1[b_idx], np.float32)
    x1own = np.ascontiguousarray(x1[b_idx, g * 1024:(g + 1) * 1024], np.float32)
    w1 = np.asarray(inputs["L1_attn_W1"], np.float32)
    w2 = np.asarray(inputs["L1_attn_W2"], np.float32)
    return dict(
        x1b=x1b, x1own=x1own,
        winT=T(inputs["L1_attn_Win"]),
        w1T=np.ascontiguousarray(w1.transpose(0, 2, 1)),
        w2T=np.ascontiguousarray(w2.transpose(0, 2, 1)),
        woutT=T(inputs["L1_attn_Wout"]),
        U1=np.ascontiguousarray(np.asarray(inputs["L1_ffn_U1"], np.float32)),
        V1=np.ascontiguousarray(np.asarray(inputs["L1_ffn_V1"], np.float32)),
        U2=np.ascontiguousarray(np.asarray(inputs["L1_ffn_U2"], np.float32)),
        V2=np.ascontiguousarray(np.asarray(inputs["L1_ffn_V2"], np.float32)),
        pbin=np.ascontiguousarray(
            np.asarray(inputs["L1_attn_bin"], np.float32).reshape(4, 128).T),
        lnw1=cols16(inputs["L1_attn_ln1_w"]),
        lnb1=cols16(inputs["L1_attn_ln1_b"]),
        lnw2=cols16(inputs["L1_attn_ln2_w"]),
        lnb2=cols16(inputs["L1_attn_ln2_b"]),
        lnlw=np.ascontiguousarray(
            np.asarray(inputs["L1_attn_lnl_w"], np.float32).reshape(4, 128).T),
        lnlb=np.ascontiguousarray(
            np.asarray(inputs["L1_attn_lnl_b"], np.float32).reshape(4, 128).T),
        prz=cols16(inputs["L1_attn_phz_re"]),
        piz=cols16(inputs["L1_attn_phz_im"]),
        lcr=cols16(inputs["L1_attn_lc_re"]),
        lci=cols16(inputs["L1_attn_lc_im"]),
        sclz=cols16(inputs["L1_attn_sc_lin"]),
        b1m=np.ascontiguousarray(
            np.asarray(inputs["L1_attn_b1"], np.float32)
            .reshape(4, 16, 128).transpose(2, 0, 1).reshape(128, 64)),
        b2m=cols16(inputs["L1_attn_b2"]),
        b1f=np.ascontiguousarray(
            np.asarray(inputs["L1_ffn_b1"], np.float32).reshape(172, 128).T),
        b2f=np.ascontiguousarray(
            np.asarray(inputs["L1_ffn_b2"], np.float32).reshape(1, D)),
        bout=np.ascontiguousarray(
            np.asarray(inputs["L1_attn_bout"], np.float32).reshape(1, D)),
        eye=np.eye(128, dtype=np.float32),
        onesc=np.ones((128, 1), dtype=np.float32),
        zrow=np.zeros((128, S), dtype=np.float32),
    )


def kernel(**inputs):
    nc_a = build_launch_a()
    in_maps = [prep_inputs_a(inputs, c // 2, c % 2) for c in range(8)]
    res = run_bass_kernel_spmd(nc_a, in_maps, core_ids=list(range(8)))
    x1 = np.zeros((4, 2048, 4096), dtype=np.float32)
    for c in range(8):
        bi, g = c // 2, c % 2
        x1[bi, g * 1024:(g + 1) * 1024] = res.results[c]["x1"]

    nc_b = build_launch_b()
    in_maps = [prep_inputs_b(inputs, x1, c // 2, c % 2) for c in range(8)]
    res = run_bass_kernel_spmd(nc_b, in_maps, core_ids=list(range(8)))

    nc_c = build_launch_c()
    in_maps = []
    for c in range(8):
        bi, g = c // 2, c % 2
        in_maps.append(dict(
            ia=np.ascontiguousarray(res.results[c]["pa"]),
            ib=np.ascontiguousarray(
                res.results[c]["atnf"][g * 1024:(g + 1) * 1024]),
        ))
    res_c = run_bass_kernel_spmd(nc_c, in_maps, core_ids=list(range(8)))
    x2 = np.zeros((4, 2048, 4096), dtype=np.float32)
    for c in range(8):
        bi, g = c // 2, c % 2
        x2[bi, g * 1024:(g + 1) * 1024] = res_c.results[c]["x2"]
    return x2
